# version 03 (final)
# baseline (speedup 1.0000x reference)
import numpy as np
import ml_dtypes
import concourse.bass as bass
import concourse.mybir as mybir
from concourse.bass_utils import run_bass_kernel_spmd

F32 = mybir.dt.float32
BF16 = mybir.dt.bfloat16
AF = mybir.ActivationFunctionType
ALU = mybir.AluOpType
AX = mybir.AxisListType

BF = ml_dtypes.bfloat16

D = 4096
TOK = 2048
TOKF = 4096
S = 4096
B = 4
H = 32
SCALE_A = float(192) ** -0.5
SCALE_B = 128.0 ** -0.5
EPS = 1e-5
NEG = -1.0e30

NCHUNK = 32
NTG = 4
ENGS = ("sync", "scalar", "vector", "tensor", "gpsimd")


class Op:
    __slots__ = ("eng", "emit", "deps", "inc_amt", "need_inc", "cum", "waits", "tag")

    def __init__(self, eng, emit, deps, inc_amt, tag):
        self.eng = eng
        self.emit = emit
        self.deps = deps
        self.inc_amt = inc_amt
        self.need_inc = False
        self.cum = 0
        self.waits = None
        self.tag = tag


class Sched:
    def __init__(self):
        self.ops = []
        self.last_w = {}
        self.readers = {}

    def add(self, eng, emit, reads=(), writes=(), dma=False, tag=""):
        deps = set()
        lw = self.last_w
        rd = self.readers
        for r in reads:
            o = lw.get(r)
            if o is not None:
                deps.add(o)
        for w in writes:
            o = lw.get(w)
            if o is not None:
                deps.add(o)
            s = rd.get(w)
            if s:
                deps.update(s)
        oid = len(self.ops)
        self.ops.append(Op(eng, emit, deps, 16 if dma else 1, tag))
        for w in writes:
            lw[w] = oid
            rd[w] = set()
        for r in reads:
            rd.setdefault(r, set()).add(oid)
        return oid

    def finalize(self):
        ops = self.ops
        for op in ops:
            kept = set()
            for d in op.deps:
                dop = ops[d]
                if dop.eng == "tensor" and op.eng == "tensor":
                    continue
                kept.add(d)
                dop.need_inc = True
            op.deps = kept
        cum = {e: 0 for e in ENGS}
        for op in ops:
            if op.need_inc:
                cum[op.eng] += op.inc_amt
            op.cum = cum[op.eng]
        waited = {}
        for op in ops:
            ws = {}
            for d in op.deps:
                dop = ops[d]
                if dop.cum > ws.get(dop.eng, 0):
                    ws[dop.eng] = dop.cum
            final = []
            for k, v in ws.items():
                kk = (op.eng, k)
                if v > waited.get(kk, 0):
                    waited[kk] = v
                    final.append((k, v))
            op.waits = final
        return cum


def rgn(buf, lo, hi):
    return [(buf, g) for g in range(lo // 512, (hi + 511) // 512)]


def build_consts():
    masks = np.zeros((4, 128, 512), dtype=np.float32)
    kk = np.arange(128)[:, None]
    qq = np.arange(128)[None, :]
    for dl in range(4):
        t = np.zeros((128, 512), dtype=np.float32)
        for m in range(4):
            keep = (128 * (m - dl) + qq - kk) >= 0
            t[:, m * 128:(m + 1) * 128] = np.where(keep, 0.0, NEG)
        masks[dl] = t
    masks = masks.reshape(4 * 128, 512)

    theta = 10000.0
    inv = 1.0 / theta ** (np.arange(0, 128, 2)[:64].astype(np.float32) / 128.0)
    pos = np.arange(TOKF)
    f = np.outer(inv, pos.astype(np.float32))
    cos = np.cos(f).astype(np.float32)
    sin = np.sin(f).astype(np.float32)
    ck = np.zeros((128, TOKF), dtype=np.float32)
    sk = np.zeros((128, TOKF), dtype=np.float32)
    ck[0::2] = cos
    ck[1::2] = cos
    sk[0::2] = -sin
    sk[1::2] = sin

    P = np.zeros((128, 128), dtype=np.float32)
    for i in range(64):
        P[2 * i, 2 * i + 1] = 1.0
        P[2 * i + 1, 2 * i] = 1.0

    sel = np.zeros((8, 128), dtype=np.float32)
    for e in range(8):
        sel[e, e * 16:(e + 1) * 16] = 1.0

    return {
        "c_masks": masks.astype(BF),
        "c_ck": ck, "c_sk": sk,
        "c_Pmat": P.astype(BF),
        "c_ident": np.eye(128, dtype=np.float32).astype(BF),
        "c_sel": sel.astype(BF),
        "c_onescol": np.ones((128, 1), dtype=np.float32).astype(BF),
        "c_onesrow": np.ones((1, 128), dtype=np.float32),
        "c_scal": np.tile(np.array([[EPS, SCALE_A, SCALE_B, 2.0, 1.0, 0.0, NEG,
                                     3.0]], dtype=np.float32), (128, 1)),
    }


def build_program(nsub=4, stage=None):
    nc = bass.Bass("TRN2", target_bir_lowering=False)

    xin = nc.dram_tensor("xin", [D, TOKF], F32, kind="ExternalInput").ap()
    W = {}
    for k, sh in {
        "a_wq": (D, 6144), "a_wkv": (D, 8256), "a_wo": (D, D),
        "b_wq": (D, D), "b_wk": (D, D), "b_wv": (D, D), "b_wo": (D, D),
        "f_base": (D, D), "f_router": (8, D),
        "f_la": (D, 128), "f_lb": (128, D),
    }.items():
        W[k] = nc.dram_tensor(k, list(sh), F32, kind="ExternalInput").ap()
    CD = {}
    for k, (sh, dt) in {
        "c_masks": ((4 * 128, 512), BF16),
        "c_ck": ((128, TOKF), F32), "c_sk": ((128, TOKF), F32),
        "c_Pmat": ((128, 128), BF16), "c_ident": ((128, 128), BF16),
        "c_sel": ((8, 128), BF16),
        "c_onescol": ((128, 1), BF16), "c_onesrow": ((1, 128), F32),
        "c_scal": ((128, 8), F32),
    }.items():
        CD[k] = nc.dram_tensor(k, list(sh), dt, kind="ExternalInput").ap()
    xout = nc.dram_tensor("xout", [D, TOKF], F32, kind="ExternalOutput").ap()

    xh = nc.dram_tensor("xh", [D, TOKF], BF16).ap()
    kv_sp = nc.dram_tensor("kv_sp", [2 * D, TOKF], BF16).ap()
    q_sp = nc.dram_tensor("q_sp", [D, TOKF], BF16).ap()
    bwt_cache = nc.dram_tensor("bwt_cache", [NCHUNK * 128, D], BF16).ap()

    big = nc.alloc_sbuf_tensor("big", [128, 65536], BF16)
    scrb = nc.alloc_sbuf_tensor("scrb", [128, 19456], BF16)
    scrf = nc.alloc_sbuf_tensor("scrf", [128, 5120], F32)
    wstg = nc.alloc_sbuf_tensor("wstg", [128, 256], F32)
    rr = nc.alloc_sbuf_tensor("rr", [128, 16], F32)
    rt8 = nc.alloc_sbuf_tensor("rt8", [128, 32], F32)
    wb8 = nc.alloc_sbuf_tensor("wb8", [128, 8], BF16)
    rwT = nc.alloc_sbuf_tensor("rwT", [128, 256], BF16)
    rstg = nc.alloc_sbuf_tensor("rstg", [8, 128], F32)
    rstgb = nc.alloc_sbuf_tensor("rstgb", [8, 128], BF16)
    pm = nc.alloc_sbuf_tensor("pm", [128, 128], BF16)
    idn = nc.alloc_sbuf_tensor("idn", [128, 128], BF16)
    selb = nc.alloc_sbuf_tensor("selb", [8, 128], BF16)
    onc = nc.alloc_sbuf_tensor("onc", [128, 1], BF16)
    onr = nc.alloc_sbuf_tensor("onr", [1, 128], F32)
    scal = nc.alloc_sbuf_tensor("scal", [128, 8], F32)
    for i, v in enumerate([EPS, SCALE_A, SCALE_B, 2.0, 1.0, 0.0, NEG, 3.0]):
        nc.const_aps.register(F32, v, scal[:, i:i + 1])

    ps = [nc.alloc_psum_tensor("ps%d" % i, [128, 512], F32) for i in range(4)]
    pstb = [nc.alloc_psum_tensor("pst%d" % i, [128, 1024], BF16) for i in range(2)]
    psb = [nc.alloc_psum_tensor("psb%d" % i, [128, 512], F32) for i in range(2)]

    SB_WQT = 0
    SB_RES = 0
    SB_WTS = 0
    SB_XT = 2048
    SB_SQ = 3072
    SB_K = 0
    SB_Q = 4096
    SB_VS = 5120
    SB_AN = 5120
    SB_VSB = 7168
    SB_EXP = 11776
    SB_MSK = 12800
    SB_WA = 5120
    SB_WB = 9216
    SB_LA = 0
    SB_LB = 13312
    SB_MID = 17408

    SF_CS = 0
    SF_XF = 0
    SF_WE = 0
    SF_LNT = 1024
    SF_T1 = 4096
    SF_T2 = 4608
    SF_XO = 4096

    sch = Sched()

    def dma(q, out, in_, reads, writes, tag=""):
        sch.add(q, lambda e, o=out, i=in_: [e.dma_start(out=o, in_=i)],
                reads=reads, writes=writes, dma=True, tag=tag)

    CREG = [("c", 0)]
    dma("sync", pm[:, :], CD["c_Pmat"][:, :], [], CREG, "cdma")
    dma("sync", idn[:, :], CD["c_ident"][:, :], [], CREG, "cdma")
    dma("sync", selb[:, :], CD["c_sel"][:, :], [], CREG, "cdma")
    dma("sync", onc[:, :], CD["c_onescol"][:, :], [], CREG, "cdma")
    dma("sync", onr[:, :], CD["c_onesrow"][:, :], [], CREG, "cdma")
    dma("sync", scal[:, :], CD["c_scal"][:, :], [], CREG, "cdma")
    for en in ENGS:
        sch.add(en, lambda e: [], reads=CREG, writes=[], tag="cbar")

    def stream_wcols(wap, col0, woff, q, scale=None):
        src3 = wap.rearrange("(c p) m -> c p m", p=128)
        for s in range(16):
            stg = [("wstg", 0)]
            for cc in range(2):
                dma(q, wstg[:, cc * 128:cc * 128 + 128],
                    src3[2 * s + cc, :, col0:col0 + 128],
                    reads=stg, writes=stg, tag="wdma")
            off = woff + 2 * s * 128
            wr = rgn("scrb", off, off + 256)

            def em(e, a=off, sc=scale):
                if sc is None:
                    return [e.copy(out=scrb[:, a:a + 256], in_=wstg[:, :])]
                return [e.mul(out=scrb[:, a:a + 256], in_=wstg[:, :], mul=sc)]
            sch.add("scalar", em, reads=stg, writes=wr, tag="wcast")
        return rgn("scrb", woff, woff + 4096)

    def get_x_tile(sub, c, half, tg, slot, q="sync"):
        r0, c0 = c * 128, half * TOK + tg * 512
        if sub == 0:
            fsl = SF_XF + slot * 512
            fr = rgn("scrf", fsl, fsl + 512)
            dma(q, scrf[:, fsl:fsl + 512], xin[r0:r0 + 128, c0:c0 + 512],
                reads=[], writes=fr, tag="xdma")
            bsl = SB_XT + slot * 512
            br = rgn("scrb", bsl, bsl + 512)
            sch.add("scalar",
                    lambda e, a=fsl, b=bsl: [e.copy(out=scrb[:, b:b + 512],
                                                    in_=scrf[:, a:a + 512])],
                    reads=fr, writes=br, tag="xcast")
            return scrb[:, bsl:bsl + 512], br
        bsl = SB_XT + slot * 512
        br = rgn("scrb", bsl, bsl + 512)
        dma(q, scrb[:, bsl:bsl + 512], xh[r0:r0 + 128, c0:c0 + 512],
            reads=[("xh", c, half * 4 + tg)], writes=br, tag="xdma")
        return scrb[:, bsl:bsl + 512], br

    def emit_ln(sub, half):
        for tg in range(NTG):
            for c in range(NCHUNK):
                xa, xr = get_x_tile(sub, c, half, tg, c % 2)
                sq = SB_SQ + (c % 2) * 512
                sqr = rgn("scrb", sq, sq + 512)
                sch.add("scalar",
                        lambda e, i=xa, a=sq: [e.square(out=scrb[:, a:a + 512],
                                                        in_=i)],
                        reads=xr, writes=sqr, tag="lnsq")
                st, sp = (c == 0), (c == NCHUNK - 1)
                sch.add("tensor",
                        lambda e, i=xa, s=st, p=sp: [
                            e.matmul(ps[0][0:1, 0:512], onc[:, :], i,
                                     start=s, stop=p)],
                        reads=list(xr) + CREG, writes=[("ps", 0)], tag="lnmm")
                sch.add("tensor",
                        lambda e, a=sq, s=st, p=sp: [
                            e.matmul(ps[1][0:1, 0:512], onc[:, :],
                                     scrb[:, a:a + 512], start=s, stop=p)],
                        reads=list(sqr) + CREG, writes=[("ps", 1)], tag="lnmm2")
            ST_M, ST_V, ST_T = 2048, 2560, 3072
            st_m = rgn("scrf", ST_M, ST_M + 512)
            st_v = rgn("scrf", ST_V, ST_V + 512)
            st_t = rgn("scrf", ST_T, ST_T + 512)
            sch.add("vector",
                    lambda e: [e.tensor_scalar(scrf[0:1, ST_M:ST_M + 512],
                                               ps[0][0:1, 0:512],
                                               1.0 / D, None, ALU.mult)],
                    reads=[("ps", 0)], writes=st_m, tag="ln1")
            sch.add("vector",
                    lambda e: [e.tensor_scalar(scrf[0:1, ST_V:ST_V + 512],
                                               ps[1][0:1, 0:512],
                                               1.0 / D, None, ALU.mult)],
                    reads=[("ps", 1)], writes=st_v, tag="ln2")
            sch.add("vector",
                    lambda e: [e.tensor_tensor(scrf[0:1, ST_T:ST_T + 512],
                                               scrf[0:1, ST_M:ST_M + 512],
                                               scrf[0:1, ST_M:ST_M + 512],
                                               ALU.mult)],
                    reads=st_m, writes=st_t, tag="ln3")
            sch.add("vector",
                    lambda e: [e.tensor_tensor(scrf[0:1, ST_V:ST_V + 512],
                                               scrf[0:1, ST_V:ST_V + 512],
                                               scrf[0:1, ST_T:ST_T + 512],
                                               ALU.subtract)],
                    reads=st_v + st_t, writes=st_v, tag="ln4")
            sch.add("scalar",
                    lambda e: [e.activation(scrf[0:1, ST_T:ST_T + 512],
                                            scrf[0:1, ST_V:ST_V + 512],
                                            AF.Sqrt, bias=EPS)],
                    reads=st_v, writes=st_t, tag="ln5")
            sch.add("vector",
                    lambda e: [e.reciprocal(scrf[0:1, ST_V:ST_V + 512],
                                            scrf[0:1, ST_T:ST_T + 512])],
                    reads=st_t, writes=st_v, tag="ln6")
            sch.add("tensor",
                    lambda e: [e.matmul(psb[0][:, 0:512], onr[:, :],
                                        scrf[0:1, ST_M:ST_M + 512],
                                        start=True, stop=True)],
                    reads=st_m + CREG, writes=[("psb0",)], tag="lnb1")
            sch.add("tensor",
                    lambda e: [e.matmul(psb[1][:, 0:512], onr[:, :],
                                        scrf[0:1, ST_V:ST_V + 512],
                                        start=True, stop=True)],
                    reads=st_v + CREG, writes=[("psb1",)], tag="lnb2")
            mb = rgn("scrf", SF_T1, SF_T1 + 512)
            rb = rgn("scrf", SF_T2, SF_T2 + 512)
            sch.add("scalar",
                    lambda e: [e.copy(out=scrf[:, SF_T1:SF_T1 + 512],
                                      in_=psb[0][:, 0:512])],
                    reads=[("psb0",)], writes=mb, tag="lnmb")
            sch.add("scalar",
                    lambda e: [e.copy(out=scrf[:, SF_T2:SF_T2 + 512],
                                      in_=psb[1][:, 0:512])],
                    reads=[("psb1",)], writes=rb, tag="lnrb")
            for c in range(NCHUNK):
                xa, xr = get_x_tile(sub, c, half, tg, c % 2)
                tf = SF_LNT + (c % 2) * 512
                tfr = rgn("scrf", tf, tf + 512)
                sch.add("vector",
                        lambda e, i=xa, a=tf: [
                            e.tensor_sub(scrf[:, a:a + 512], i,
                                         scrf[:, SF_T1:SF_T1 + 512])],
                        reads=list(xr) + mb, writes=tfr, tag="lnsub")
                lo = c * 2048 + tg * 512
                lr = rgn("big", lo, lo + 512)
                sch.add("vector",
                        lambda e, a=tf, b=lo: [
                            e.tensor_mul(big[:, b:b + 512], scrf[:, a:a + 512],
                                         scrf[:, SF_T2:SF_T2 + 512])],
                        reads=list(tfr) + rb, writes=lr, tag="lnmul")

    def proj_colblock(wap, wcol0, cb_id, dest_fn, scale=None, rope=False):
        woff = SB_WA if (cb_id % 2 == 0) else SB_WB
        wr_all = stream_wcols(wap, wcol0, woff, "sync")
        for tg in range(NTG):
            pk = ("ps", tg % 2)

            def mm_em(e, w=woff, t=tg):
                o = []
                for c in range(NCHUNK):
                    o.append(e.matmul(
                        ps[t % 2][:, 0:512],
                        scrb[:, w + c * 128:w + (c + 1) * 128],
                        big[:, c * 2048 + t * 512: c * 2048 + t * 512 + 512],
                        start=(c == 0), stop=(c == NCHUNK - 1)))
                return o
            lr = []
            for c in range(NCHUNK):
                lo = c * 2048 + tg * 512
                lr += rgn("big", lo, lo + 512)
            sch.add("tensor", mm_em, reads=wr_all + lr, writes=[pk], tag="projmm")
            qsl = SB_WQT + (tg % 2) * 512
            qr = rgn("scrb", qsl, qsl + 512)

            def ev_em(e, t=tg, a=qsl, sc=scale):
                src = ps[t % 2][:, 0:512]
                if sc is None:
                    return [e.copy(out=scrb[:, a:a + 512], in_=src)]
                return [e.mul(out=scrb[:, a:a + 512], in_=src, mul=sc)]
            sch.add("scalar", ev_em, reads=[pk] + CREG, writes=qr, tag="projev")
            if not rope:
                dest_fn(tg, scrb[:, qsl:qsl + 512], qr)
                continue
            rotk = ("psb%d" % (tg % 2),)
            sch.add("tensor",
                    lambda e, a=qsl, t=tg: [
                        e.matmul(psb[t % 2][:, 0:512], pm[:, :],
                                 scrb[:, a:a + 512], start=True, stop=True)],
                    reads=list(qr) + CREG, writes=[rotk], tag="ropemm")
            csr = rgn("scrf", SF_CS + tg * 512, SF_CS + tg * 512 + 512)
            snr = rgn("scrf", SF_CS + 2048 + tg * 512, SF_CS + 2048 + tg * 512 + 512)
            t1r = rgn("scrf", SF_T1, SF_T1 + 512)
            t2r = rgn("scrf", SF_T2, SF_T2 + 512)
            sch.add("vector",
                    lambda e, a=qsl, t=tg: [
                        e.tensor_mul(scrf[:, SF_T1:SF_T1 + 512], scrb[:, a:a + 512],
                                     scrf[:, SF_CS + t * 512:SF_CS + t * 512 + 512])],
                    reads=list(qr) + csr, writes=t1r, tag="ropec")
            sch.add("vector",
                    lambda e, t=tg: [
                        e.tensor_mul(scrf[:, SF_T2:SF_T2 + 512],
                                     psb[t % 2][:, 0:512],
                                     scrf[:, SF_CS + 2048 + t * 512:
                                          SF_CS + 2048 + t * 512 + 512])],
                    reads=[rotk] + snr, writes=t2r, tag="ropes")
            osl = SB_WQT + 1024 + (tg % 2) * 512
            orr = rgn("scrb", osl, osl + 512)
            sch.add("vector",
                    lambda e, a=osl: [
                        e.tensor_add(scrb[:, a:a + 512], scrf[:, SF_T1:SF_T1 + 512],
                                     scrf[:, SF_T2:SF_T2 + 512])],
                    reads=t1r + t2r, writes=orr, tag="ropea")
            dest_fn(tg, scrb[:, osl:osl + 512], orr)

    def emit_outproj(sub, half, wap, extra_mm):
        last = (sub == 3)
        for db in range(NCHUNK):
            woff = SB_WA if (db % 2 == 0) else SB_WB
            if extra_mm is None:
                wr_all = stream_wcols(wap, db * 128, woff, "sync")
            else:
                wr_all = load_basewT(sub, half, db, woff)
            for tg in range(NTG):
                pk = ("ps", tg % 2)

                def mm_em(e, w=woff, t=tg, db_=db):
                    o = []
                    for c in range(NCHUNK):
                        laststop = (c == NCHUNK - 1) and (extra_mm is None)
                        o.append(e.matmul(
                            ps[t % 2][:, 0:512],
                            scrb[:, w + c * 128:w + (c + 1) * 128],
                            big[:, c * 2048 + t * 512: c * 2048 + t * 512 + 512],
                            start=(c == 0), stop=laststop))
                    if extra_mm is not None:
                        o += extra_mm(e, db_, t)
                    return o
                lr = []
                for c in range(NCHUNK):
                    lo = c * 2048 + tg * 512
                    lr += rgn("big", lo, lo + 512)
                extra_r = []
                if extra_mm is not None:
                    extra_r = rgn("scrb", SB_MID, SB_MID + 2048) + \
                        rgn("scrb", SB_LB, SB_LB + 4096)
                sch.add("tensor", mm_em, reads=wr_all + lr + extra_r,
                        writes=[pk], tag="outmm")
                r0, c0 = db * 128, half * TOK + tg * 512
                if sub == 0:
                    fsl = SF_XF + (tg % 2) * 512
                    xr = rgn("scrf", fsl, fsl + 512)
                    dma("scalar", scrf[:, fsl:fsl + 512],
                        xin[r0:r0 + 128, c0:c0 + 512],
                        reads=xr, writes=xr, tag="resdma")
                    xa = scrf[:, fsl:fsl + 512]
                else:
                    bsl = SB_RES + (tg % 2) * 512
                    xr = rgn("scrb", bsl, bsl + 512)
                    dma("scalar", scrb[:, bsl:bsl + 512],
                        xh[r0:r0 + 128, c0:c0 + 512],
                        reads=[("xh", db, half * 4 + tg)] + xr, writes=xr,
                        tag="resdma")
                    xa = scrb[:, bsl:bsl + 512]
                if last:
                    osl = SF_XO + (tg % 2) * 512
                    orr = rgn("scrf", osl, osl + 512)
                    sch.add("vector",
                            lambda e, t=tg, x=xa, o=osl: [
                                e.tensor_add(scrf[:, o:o + 512],
                                             ps[t % 2][:, 0:512], x)],
                            reads=[pk] + list(xr), writes=orr, tag="resadd")
                    dma("scalar", xout[r0:r0 + 128, c0:c0 + 512],
                        scrf[:, osl:osl + 512],
                        reads=orr, writes=[("xout", db, half * 4 + tg)],
                        tag="xoutdma")
                else:
                    osl = SB_RES + 1024 + (tg % 2) * 512
                    orr = rgn("scrb", osl, osl + 512)
                    sch.add("vector",
                            lambda e, t=tg, x=xa, o=osl: [
                                e.tensor_add(scrb[:, o:o + 512],
                                             ps[t % 2][:, 0:512], x)],
                            reads=[pk] + list(xr), writes=orr, tag="resadd")
                    dma("scalar", xh[r0:r0 + 128, c0:c0 + 512],
                        scrb[:, osl:osl + 512],
                        reads=orr, writes=[("xh", db, half * 4 + tg)],
                        tag="xhdma")

    def emit_attn(sub, kind):
        a = (kind == "a")
        for half in range(2):
            emit_ln(sub, half)
            if not a:
                dma("sync", scrf[:, SF_CS:SF_CS + 2048],
                    CD["c_ck"][:, half * TOK:(half + 1) * TOK],
                    reads=rgn("scrf", SF_CS, SF_CS + 2048),
                    writes=rgn("scrf", SF_CS, SF_CS + 2048), tag="csdma")
                dma("sync", scrf[:, SF_CS + 2048:SF_CS + 4096],
                    CD["c_sk"][:, half * TOK:(half + 1) * TOK],
                    reads=rgn("scrf", SF_CS + 2048, SF_CS + 4096),
                    writes=rgn("scrf", SF_CS + 2048, SF_CS + 4096), tag="csdma")
            for cb in range(64):
                if a:
                    wap = W["a_wkv"]
                    wcol = cb * 128 if cb < 32 else 4096 + (cb - 32) * 128
                else:
                    wap = W["b_wk"] if cb < 32 else W["b_wv"]
                    wcol = (cb % 32) * 128
                rope = (not a) and cb < 32

                def mk_dest(cbi, hf):
                    def dest(tg, src, srcr):
                        dma("gpsimd",
                            kv_sp[cbi * 128:(cbi + 1) * 128,
                                  hf * TOK + tg * 512:hf * TOK + (tg + 1) * 512],
                            src, reads=srcr, writes=[("kvo", cbi, hf)],
                            tag="kvout")
                    return dest
                proj_colblock(wap, wcol, cb, mk_dest(cb, half), rope=rope)
            for cb in range(32):
                if a:
                    wap, wcol, sc, rope = W["a_wq"], cb * 192, SCALE_A, False
                else:
                    wap, wcol, sc, rope = W["b_wq"], cb * 128, SCALE_B, True

                def mk_dest(cbi, hf):
                    def dest(tg, src, srcr):
                        dma("gpsimd",
                            q_sp[cbi * 128:(cbi + 1) * 128,
                                 hf * TOK + tg * 512:hf * TOK + (tg + 1) * 512],
                            src, reads=srcr, writes=[("qsp", cbi, hf)],
                            tag="qout")
                    return dest
                proj_colblock(wap, wcol, cb, mk_dest(cb, half), scale=sc,
                              rope=rope)

        for half in range(2):
            klen = (half + 1) * TOK
            nkt = klen // 128
            mr = rgn("scrb", SB_MSK, SB_MSK + 2048)
            dma("sync",
                scrb[:, SB_MSK:SB_MSK + 2048].rearrange("p (i m) -> p i m", i=4),
                CD["c_masks"].rearrange("(i p) m -> p i m", p=128),
                reads=mr, writes=mr, tag="mskdma")
            vsbr = rgn("scrb", SB_VSB, SB_VSB + 4128)
            sch.add("vector",
                    lambda e: [e.memset(scrb[:, SB_VSB:SB_VSB + 4128], 1.0)],
                    reads=vsbr, writes=vsbr, tag="vones")
            for h in range(H):
                kvr = [("kvo", h, hf) for hf in range(half + 1)]
                vvr = [("kvo", 32 + h, hf) for hf in range(half + 1)]
                kr = rgn("scrb", SB_K, SB_K + klen)
                dma("gpsimd", scrb[:, SB_K:SB_K + klen],
                    kv_sp[h * 128:(h + 1) * 128, 0:klen],
                    reads=kvr + kr, writes=kr, tag="kdma")
                for vh in range(half + 1):
                    vsr = rgn("scrb", SB_VS, SB_VS + 2048)
                    dma("gpsimd", scrb[:, SB_VS:SB_VS + 2048],
                        kv_sp[D + h * 128:D + (h + 1) * 128,
                              vh * TOK:(vh + 1) * TOK],
                        reads=vvr + vsr, writes=vsr, tag="vdma")
                    for jj in range(16):
                        j = vh * 16 + jj
                        bk = j % 2
                        so = ((j >> 1) % 4) * 128
                        pr = ("pst%d" % bk,)
                        sch.add("tensor",
                                lambda e, o=jj, b_=bk, s_=so: [
                                    e.transpose(pstb[b_][:, s_:s_ + 128],
                                                scrb[:, SB_VS + o * 128:
                                                     SB_VS + (o + 1) * 128],
                                                idn[:, :])],
                                reads=vsr + CREG, writes=[pr], tag="vtr")
                        vs0 = SB_VSB + j * 129
                        sch.add("scalar",
                                lambda e, a=vs0, b_=bk, s_=so: [
                                    e.copy(out=scrb[:, a:a + 128],
                                           in_=pstb[b_][:, s_:s_ + 128])],
                                reads=[pr], writes=rgn("scrb", vs0, vs0 + 129),
                                tag="vev")
                for g in range(NTG):
                    jmax = half * 16 + 4 * g + 4
                    qsl = SB_Q + (g % 2) * 512
                    qrg = rgn("scrb", qsl, qsl + 512)
                    dma("gpsimd", scrb[:, qsl:qsl + 512],
                        q_sp[h * 128:(h + 1) * 128,
                             half * TOK + g * 512:half * TOK + (g + 1) * 512],
                        reads=[("qsp", h, half)] + qrg, writes=qrg, tag="qdma")
                    pvk = [("ps2",), ("ps3",)]
                    for ti in range(jmax):
                        j = ti
                        koff = SB_K + j * 128
                        stk = ("ps", ti % 2)
                        sch.add("tensor",
                                lambda e, ko=koff, q_=qsl, s_=ti % 2: [
                                    e.matmul(ps[s_][:, 0:512],
                                             scrb[:, ko:ko + 128],
                                             scrb[:, q_:q_ + 512],
                                             start=True, stop=True)],
                                reads=kr + qrg, writes=[stk], tag="st")
                        dl = j - (half * 16 + 4 * g)
                        if dl >= 0:
                            mo = SB_MSK + dl * 512
                            sch.add("vector",
                                    lambda e, s_=ti % 2, m=mo: [
                                        e.tensor_add(ps[s_][:, 0:512],
                                                     ps[s_][:, 0:512],
                                                     scrb[:, m:m + 512])],
                                    reads=[stk] + mr, writes=[stk], tag="mask")
                        ex = SB_EXP + (ti % 2) * 512
                        exr = rgn("scrb", ex, ex + 512)
                        sch.add("scalar",
                                lambda e, s_=ti % 2, a=ex: [
                                    e.activation(scrb[:, a:a + 512],
                                                 ps[s_][:, 0:512], AF.Exp)],
                                reads=[stk] + CREG, writes=exr, tag="exp")
                        vs0 = SB_VSB + j * 129
                        fi, la = (ti == 0), (ti == jmax - 1)

                        def pv_em(e, a=ex, v0=vs0, f_=fi, l_=la):
                            o = []
                            for m in range(4):
                                bank = ps[2 + m // 2]
                                c0 = (m % 2) * 256
                                o.append(e.matmul(
                                    bank[:, c0:c0 + 129],
                                    scrb[:, a + m * 128:a + (m + 1) * 128],
                                    scrb[:, v0:v0 + 129],
                                    start=(f_ and m % 2 == 0), stop=l_))
                            return o
                        sch.add("tensor", pv_em,
                                reads=exr + rgn("scrb", vs0, vs0 + 129),
                                writes=pvk, tag="pv")
                    for m in range(4):
                        bank = ps[2 + m // 2]
                        bkey = ("ps2",) if m < 2 else ("ps3",)
                        c0 = (m % 2) * 256
                        sch.add("vector",
                                lambda e, b=bank, cc=c0, mm=m: [
                                    e.reciprocal(rr[:, mm:mm + 1],
                                                 b[:, cc + 128:cc + 129])],
                                reads=[bkey], writes=[("rr", m)], tag="rcp")
                        ansl = SB_AN + (m % 2) * 128
                        ank = rgn("scrb", ansl, ansl + 128)
                        sch.add("vector",
                                lambda e, b=bank, cc=c0, mm=m, o=ansl: [
                                    e.tensor_scalar(scrb[:, o:o + 128],
                                                    b[:, cc:cc + 128],
                                                    rr[:, mm:mm + 1], None,
                                                    ALU.mult)],
                                reads=[bkey, ("rr", m)], writes=ank,
                                tag="annorm")
                        bk = m % 2
                        sch.add("tensor",
                                lambda e, o=ansl, b_=bk: [
                                    e.transpose(pstb[b_][:, 512:640],
                                                scrb[:, o:o + 128], idn[:, :])],
                                reads=ank + CREG, writes=[("pst%d" % bk,)],
                                tag="antr")
                        ao = h * 2048 + (4 * g + m) * 128
                        sch.add("scalar",
                                lambda e, aa=ao, b_=bk: [
                                    e.copy(out=big[:, aa:aa + 128],
                                           in_=pstb[b_][:, 512:640])],
                                reads=[("pst%d" % bk,)],
                                writes=rgn("big", ao, ao + 128), tag="anout")
            emit_outproj(sub, half, W["a_wo"] if a else W["b_wo"], None)

    def load_basewT(sub, half, ob, woff):
        wr_all = rgn("scrb", woff, woff + 4096)
        if sub == 3 or half == 1:
            dma("sync", scrb[:, woff:woff + 4096],
                bwt_cache[ob * 128:(ob + 1) * 128, :],
                reads=[("bwt", ob)] + wr_all, writes=wr_all, tag="bwtdma")
            return wr_all
        noff = SB_WB if woff == SB_WA else SB_WA
        natr = rgn("scrb", noff, noff + 4096)
        src = W["f_base"][ob * 128:(ob + 1) * 128, :]
        for s in range(16):
            stg = [("wstg", 0)]
            dma("sync", wstg[:, :], src[:, s * 256:(s + 1) * 256],
                reads=stg, writes=stg, tag="bwdma")
            off = noff + s * 256
            sch.add("scalar",
                    lambda e, a=off: [e.copy(out=scrb[:, a:a + 256],
                                             in_=wstg[:, :])],
                    reads=stg, writes=rgn("scrb", off, off + 256), tag="bwcast")
        for c in range(NCHUNK):
            bk = c % 2
            so = ((c >> 1) % 4) * 128
            sch.add("tensor",
                    lambda e, cc=c, b_=bk, s_=so, no=noff: [
                        e.transpose(pstb[b_][:, s_:s_ + 128],
                                    scrb[:, no + cc * 128:no + (cc + 1) * 128],
                                    idn[:, :])],
                    reads=natr + CREG, writes=[("pst%d" % bk,)], tag="bwtr")
            sch.add("scalar",
                    lambda e, cc=c, b_=bk, s_=so, wo=woff: [
                        e.copy(out=scrb[:, wo + cc * 128:wo + (cc + 1) * 128],
                               in_=pstb[b_][:, s_:s_ + 128])],
                    reads=[("pst%d" % bk,)],
                    writes=rgn("scrb", woff + c * 128, woff + (c + 1) * 128),
                    tag="bwev")
        dma("sync", bwt_cache[ob * 128:(ob + 1) * 128, :],
            scrb[:, woff:woff + 4096],
            reads=wr_all, writes=[("bwt", ob)], tag="bwtw")
        return wr_all

    def emit_ffn(sub):
        for half in range(2):
            emit_ln(sub, half)
            for c in range(NCHUNK):
                sr = [("rstg", 0)]
                dma("sync", rstg[:, :], W["f_router"][:, c * 128:(c + 1) * 128],
                    reads=sr, writes=sr, tag="rwdma")
                sch.add("scalar",
                        lambda e: [e.copy(out=rstgb[:, :], in_=rstg[:, :])],
                        reads=sr, writes=[("rstgb",)], tag="rwcast")
                sch.add("tensor",
                        lambda e: [e.transpose(pstb[0][:, 0:8], rstgb[:, :],
                                               idn[0:8, 0:8])],
                        reads=[("rstgb",)] + CREG, writes=[("pst0",)], tag="rwtr")
                sch.add("scalar",
                        lambda e, cc=c: [e.copy(out=rwT[:, cc * 8:(cc + 1) * 8],
                                                in_=pstb[0][:, 0:8])],
                        reads=[("pst0",)], writes=[("rwT",)], tag="rwev")
            for tb in range(16):
                def lg_em(e, t=tb):
                    o = []
                    for c in range(NCHUNK):
                        o.append(e.matmul(
                            psb[0][:, 0:8],
                            big[:, c * 2048 + t * 128: c * 2048 + t * 128 + 128],
                            rwT[:, c * 8:(c + 1) * 8],
                            start=(c == 0), stop=(c == NCHUNK - 1)))
                    return o
                lr = []
                for c in range(NCHUNK):
                    lo = c * 2048 + tb * 128
                    lr += rgn("big", lo, lo + 128)
                sch.add("tensor", lg_em, reads=lr + [("rwT",)],
                        writes=[("psb0",)], tag="lgmm")
                sch.add("vector",
                        lambda e: [e.reduce_max(rr[:, 8:9], psb[0][:, 0:8], AX.X)],
                        reads=[("psb0",)], writes=[("rr", 8)], tag="rmax")
                sch.add("vector",
                        lambda e: [e.tensor_scalar(rr[:, 9:10], rr[:, 8:9], -1.0,
                                                   None, ALU.mult)],
                        reads=[("rr", 8)], writes=[("rr", 9)], tag="rneg")
                sch.add("scalar",
                        lambda e: [e.activation(rt8[:, 0:8], psb[0][:, 0:8],
                                                AF.Exp, bias=rr[:, 9:10])],
                        reads=[("psb0",), ("rr", 9)] + CREG, writes=[("rt8", 0)],
                        tag="rexp")
                sch.add("vector",
                        lambda e: [e.reduce_max(rr[:, 10:11], rt8[:, 0:8], AX.X)],
                        reads=[("rt8", 0)], writes=[("rr", 10)], tag="rm1")
                sch.add("vector",
                        lambda e: [e.tensor_scalar(rt8[:, 8:16], rt8[:, 0:8],
                                                   rr[:, 10:11], None, ALU.is_lt)],
                        reads=[("rt8", 0), ("rr", 10)], writes=[("rt8", 1)],
                        tag="rlt")
                sch.add("vector",
                        lambda e: [e.tensor_mul(rt8[:, 16:24], rt8[:, 0:8],
                                                rt8[:, 8:16])],
                        reads=[("rt8", 0), ("rt8", 1)], writes=[("rt8", 2)],
                        tag="rq2")
                sch.add("vector",
                        lambda e: [e.reduce_max(rr[:, 11:12], rt8[:, 16:24],
                                                AX.X)],
                        reads=[("rt8", 2)], writes=[("rr", 11)], tag="rs")
                sch.add("vector",
                        lambda e: [e.tensor_tensor(rr[:, 12:13], rr[:, 10:11],
                                                   rr[:, 11:12], ALU.add)],
                        reads=[("rr", 10), ("rr", 11)], writes=[("rr", 12)],
                        tag="rden")
                sch.add("vector",
                        lambda e: [e.reciprocal(rr[:, 13:14], rr[:, 12:13])],
                        reads=[("rr", 12)], writes=[("rr", 13)], tag="rrec")
                sch.add("vector",
                        lambda e: [e.tensor_scalar(rt8[:, 8:16], rt8[:, 0:8],
                                                   rr[:, 11:12], None, ALU.is_ge)],
                        reads=[("rt8", 0), ("rr", 11)], writes=[("rt8", 1)],
                        tag="rge")
                sch.add("vector",
                        lambda e: [e.tensor_mul(rt8[:, 16:24], rt8[:, 0:8],
                                                rt8[:, 8:16])],
                        reads=[("rt8", 0), ("rt8", 1)], writes=[("rt8", 2)],
                        tag="rwu")
                sch.add("vector",
                        lambda e: [e.tensor_scalar(rt8[:, 24:32], rt8[:, 16:24],
                                                   rr[:, 13:14], None, ALU.mult)],
                        reads=[("rt8", 2), ("rr", 13)], writes=[("rt8", 3)],
                        tag="rwf")
                sch.add("scalar",
                        lambda e: [e.copy(out=wb8[:, :], in_=rt8[:, 24:32])],
                        reads=[("rt8", 3)], writes=[("wb8",)], tag="rwb")
                sch.add("tensor",
                        lambda e: [e.transpose(pstb[1][0:8, 0:128], wb8[:, :],
                                               idn[:, :])],
                        reads=[("wb8",)] + CREG, writes=[("pst1",)], tag="wtr")
                sch.add("scalar",
                        lambda e, t=tb: [
                            e.copy(out=scrb[0:8,
                                            SB_WTS + t * 128:SB_WTS + (t + 1) * 128],
                                   in_=pstb[1][0:8, 0:128])],
                        reads=[("pst1",)],
                        writes=rgn("scrb", SB_WTS + tb * 128,
                                   SB_WTS + (tb + 1) * 128),
                        tag="wev")
            for tg in range(NTG):
                wtr_ = rgn("scrb", tg * 512, (tg + 1) * 512)
                sch.add("tensor",
                        lambda e, t=tg: [
                            e.matmul(psb[0][:, 0:512], selb[:, :],
                                     scrb[0:8,
                                          SB_WTS + t * 512:SB_WTS + (t + 1) * 512],
                                     start=True, stop=True)],
                        reads=wtr_ + CREG, writes=[("psb0",)], tag="selmm")
                wek = rgn("scrf", SF_WE + tg * 512, SF_WE + tg * 512 + 512)
                sch.add("scalar",
                        lambda e, t=tg: [
                            e.copy(out=scrf[:, SF_WE + t * 512:
                                            SF_WE + t * 512 + 512],
                                   in_=psb[0][:, 0:512])],
                        reads=[("psb0",)], writes=wek, tag="weev")
            la3 = W["f_la"].rearrange("(c p) m -> c p m", p=128)
            for s in range(16):
                stg = [("wstg", 0)]
                for cc in range(2):
                    dma("sync", wstg[:, cc * 128:cc * 128 + 128],
                        la3[2 * s + cc, :, :], reads=stg, writes=stg, tag="ladma")
                off = SB_LA + s * 256
                sch.add("scalar",
                        lambda e, a=off: [e.copy(out=scrb[:, a:a + 256],
                                                 in_=wstg[:, :])],
                        reads=stg, writes=rgn("scrb", off, off + 256),
                        tag="lacast")
            lar = rgn("scrb", SB_LA, SB_LA + 4096)
            for tg in range(NTG):
                pk = ("ps", tg % 2)

                def mid_em(e, t=tg):
                    o = []
                    for c in range(NCHUNK):
                        o.append(e.matmul(
                            ps[t % 2][:, 0:512],
                            scrb[:, SB_LA + c * 128:SB_LA + (c + 1) * 128],
                            big[:, c * 2048 + t * 512: c * 2048 + t * 512 + 512],
                            start=(c == 0), stop=(c == NCHUNK - 1)))
                    return o
                lr = []
                for c in range(NCHUNK):
                    lo = c * 2048 + tg * 512
                    lr += rgn("big", lo, lo + 512)
                sch.add("tensor", mid_em, reads=lar + lr, writes=[pk],
                        tag="midmm")
                wek = rgn("scrf", SF_WE + tg * 512, SF_WE + tg * 512 + 512)
                mo = SB_MID + tg * 512
                sch.add("vector",
                        lambda e, t=tg, m=mo: [
                            e.tensor_mul(scrb[:, m:m + 512], ps[t % 2][:, 0:512],
                                         scrf[:, SF_WE + t * 512:
                                              SF_WE + t * 512 + 512])],
                        reads=[pk] + wek, writes=rgn("scrb", mo, mo + 512),
                        tag="midsc")
            for s in range(16):
                stg = [("wstg", 0)]
                dma("sync", wstg[:, :], W["f_lb"][:, s * 256:(s + 1) * 256],
                    reads=stg, writes=stg, tag="lbdma")
                off = SB_LB + s * 256
                sch.add("scalar",
                        lambda e, a=off: [e.mul(out=scrb[:, a:a + 256],
                                                in_=wstg[:, :], mul=2.0)],
                        reads=stg + CREG, writes=rgn("scrb", off, off + 256),
                        tag="lbcast")

            def extra_mm(e, db, tg):
                return [e.matmul(ps[tg % 2][:, 0:512],
                                 scrb[:, SB_LB + db * 128:SB_LB + (db + 1) * 128],
                                 scrb[:, SB_MID + tg * 512:
                                      SB_MID + tg * 512 + 512],
                                 start=False, stop=True)]
            emit_outproj(sub, half, None, extra_mm)

    if stage is not None:
        if stage >= 1:
            emit_ln(0, 0)
        if stage >= 2:
            for cb in range(4):
                def mk_dest(cbi, hf):
                    def dest(tg, src, srcr):
                        dma("gpsimd",
                            kv_sp[cbi * 128:(cbi + 1) * 128,
                                  hf * TOK + tg * 512:hf * TOK + (tg + 1) * 512],
                            src, reads=srcr, writes=[("kvo", cbi, hf)],
                            tag="kvout")
                    return dest
                proj_colblock(W["a_wkv"], cb * 128, cb, mk_dest(cb, 0))
    else:
        emit_attn(0, "a")
    if nsub > 1:
        emit_ffn(1)
    if nsub > 2:
        emit_attn(2, "b")
    if nsub > 3:
        emit_ffn(3)

    if nsub > 3:
        xor_ = [("xout", db, hg) for db in range(NCHUNK) for hg in range(8)]
    elif stage is None:
        xor_ = [("xh", db, hg) for db in range(NCHUNK) for hg in range(8)]
    else:
        xor_ = [("kvo", cb, 0) for cb in range(4)] if stage >= 2 else [("big", 0)]
    for en in ENGS:
        sch.add(en, lambda e: [], reads=xor_, writes=[], tag="drain")

    sch.finalize()

    per_engine = {k: [] for k in ENGS}
    for op in sch.ops:
        per_engine[op.eng].append(op)
    sems = {en: nc.alloc_semaphore("sem_" + en) for en in ENGS}

    def mk_body(en):
        def body(eng):
            for op in per_engine[en]:
                for (pe, v) in op.waits:
                    eng.wait_ge(sems[pe], v)
                insts = op.emit(eng)
                if op.need_inc:
                    insts[-1].then_inc(sems[en], op.inc_amt)
        return body

    with nc.Block() as block:
        block.sync(mk_body("sync"))
        block.scalar(mk_body("scalar"))
        block.vector(mk_body("vector"))
        block.tensor(mk_body("tensor"))
        block.gpsimd(mk_body("gpsimd"))
    return nc


def kernel(L0_attn_a_hidden_states, L0_attn_a_wq, L0_attn_a_wkv, L0_attn_a_wo,
           L0_attn_b_wq, L0_attn_b_wk, L0_attn_b_wv, L0_attn_b_wo,
           L0_ffn_base_W, L0_ffn_router_W, L0_ffn_lora_A, L0_ffn_lora_B):
    x = np.asarray(L0_attn_a_hidden_states, dtype=np.float32)
    ws = {
        "a_wq": np.ascontiguousarray(np.asarray(L0_attn_a_wq, dtype=np.float32)),
        "a_wkv": np.ascontiguousarray(np.asarray(L0_attn_a_wkv, dtype=np.float32)),
        "a_wo": np.ascontiguousarray(np.asarray(L0_attn_a_wo, dtype=np.float32)),
        "b_wq": np.ascontiguousarray(np.asarray(L0_attn_b_wq, dtype=np.float32)),
        "b_wk": np.ascontiguousarray(np.asarray(L0_attn_b_wk, dtype=np.float32)),
        "b_wv": np.ascontiguousarray(np.asarray(L0_attn_b_wv, dtype=np.float32)),
        "b_wo": np.ascontiguousarray(np.asarray(L0_attn_b_wo, dtype=np.float32)),
        "f_base": np.ascontiguousarray(np.asarray(L0_ffn_base_W, dtype=np.float32)),
        "f_router": np.ascontiguousarray(
            np.asarray(L0_ffn_router_W, dtype=np.float32)),
        "f_la": np.ascontiguousarray(
            np.asarray(L0_ffn_lora_A, dtype=np.float32).transpose(1, 0, 2)
            .reshape(D, 128)),
        "f_lb": np.ascontiguousarray(
            np.asarray(L0_ffn_lora_B, dtype=np.float32).reshape(128, D)),
    }

    nc = build_program()
    consts = build_consts()
    in_maps = []
    for core in range(8):
        b = core % 4
        m = {"xin": np.ascontiguousarray(x[b].T)}
        m.update(ws)
        m.update(consts)
        in_maps.append(m)

    res = run_bass_kernel_spmd(nc, in_maps, core_ids=list(range(8)))

    out = np.empty((B, S, D), dtype=np.float32)
    for b in range(B):
        xo = np.asarray(res.results[b]["xout"])
        out[b] = np.ascontiguousarray(xo.T)
    return out
